# version 01
# baseline (speedup 1.0000x reference)
import contextlib
import numpy as np
import concourse.bass as bass
import concourse.mybir as mybir
from concourse.bass_utils import run_bass_kernel_spmd

F32 = mybir.dt.float32
BF16 = mybir.dt.bfloat16
AF = mybir.ActivationFunctionType
ALU = mybir.AluOpType

NCORE = 8
D = 2048
DFF = 8192
NCH = 16
FCH = 64
T = 2048
B = 4
S = 4096
EPS = 1e-5

ENGINES = ("sync", "gpsimd", "vector", "scalar", "tensor")
AUTO = ("vector", "scalar")


class Sem:
    __slots__ = ("h", "v")

    def __init__(self, h):
        self.h = h
        self.v = 0


class Prog:
    def __init__(self):
        self.q = {e: [] for e in ENGINES}
        self.waits = []
        self.auto = {}

    def emit(self, eng, fn):
        self.q[eng].append(fn)

    def wait(self, eng, sem, val):
        if val <= 0:
            return
        self.waits.append((sem, val))
        self.emit(eng, lambda E, s=sem.h, v=val: E.wait_ge(s, v))

    def dma(self, eng, out, in_, sem):
        self.emit(eng, lambda E, o=out, i=in_, s=sem.h: E.dma_start(out=o, in_=i).then_inc(s, 16))
        sem.v += 16
        return sem.v

    def op(self, eng, fn, sem=None, inc=1):
        if eng in self.auto:
            assert sem is None
            s = self.auto[eng]
            if s.v > 0:
                self.wait(eng, s, s.v)
            self.emit(eng, lambda E, f=fn, sh=s.h: f(E).then_inc(sh, 1))
            s.v += 1
            return s.v
        if sem is None:
            self.emit(eng, lambda E, f=fn: f(E))
            return None
        self.emit(eng, lambda E, f=fn, sh=sem.h, n=inc: f(E).then_inc(sh, n))
        sem.v += inc
        return sem.v

    def wv(self, eng, val):
        self.wait(eng, self.auto["vector"], val)

    def wsc(self, eng, val):
        self.wait(eng, self.auto["scalar"], val)

    def finalize(self, block):
        for s, v in self.waits:
            assert v <= s.v, f"wait for {v} but final count {s.v}"
        for eng in ENGINES:
            fns = self.q[eng]
            if not fns:
                continue
            getattr(block, eng)(lambda E, fns=fns: [f(E) for f in fns])


class DSem:

    def __init__(self, c, name, slots=2):
        self.sems = [c.sem(f"{name}_{i}") for i in range(slots)]
        self.n = 0

    def issue(self, P, eng, out, in_):
        s = self.sems[self.n % len(self.sems)]
        P.dma(eng, out, in_, s)
        tok = (self.n % len(self.sems), s.v)
        self.n += 1
        return tok

    def wait(self, P, eng, tok):
        P.wait(eng, self.sems[tok[0]], tok[1])

    def wait_all(self, P, eng):
        for s in self.sems:
            if s.v:
                P.wait(eng, s, s.v)


class Ctx:
    def __init__(self):
        self.nc = bass.Bass("TRN2", target_bir_lowering=False, debug=False)
        self.es = contextlib.ExitStack()
        self.P = Prog()
        self.P.auto = {"vector": self.sem("vself"), "scalar": self.sem("scself")}

    def sem(self, name):
        return Sem(self.es.enter_context(self.nc.semaphore(name)))

    def dsem(self, name, slots=2):
        return DSem(self, name, slots)

    def sb(self, name, shape, dtype):
        return self.es.enter_context(self.nc.sbuf_tensor(name, shape, dtype))

    def ps(self, name, shape, dtype=F32):
        return self.es.enter_context(self.nc.psum_tensor(name, shape, dtype))

    def din(self, name, shape, dtype=F32):
        return self.nc.dram_tensor(name, shape, dtype, kind="ExternalInput").ap()

    def dout(self, name, shape, dtype=F32):
        return self.nc.dram_tensor(name, shape, dtype, kind="ExternalOutput").ap()

    def dtmp(self, name, shape, dtype):
        return self.nc.dram_tensor(name, shape, dtype).ap()

    def build(self):
        with self.es:
            block = self.es.enter_context(self.nc.Block())
            self.P.finalize(block)
        return self.nc


def wview(w):
    return w.rearrange("(c p) m -> c p m", p=128)


def pview(w):
    return w.rearrange("(c p) m -> p c m", p=128)



def make_ws(c, atmp=False):
    ws = dict(
        ones=c.sb("ws_ones", [128, 128], BF16),
        sq=[c.sb("ws_sq0", [128, T], BF16)],
        m32=c.sb("ws_m32", [128, T], F32),
        v32=c.sb("ws_v32", [128, T], F32),
        w32=c.sb("ws_w32", [128, T], F32),
        rb=c.sb("ws_rb", [128, T], BF16),
        mrb=c.sb("ws_mrb", [128, T], BF16),
        eps=c.sb("ws_eps", [128, 1], F32),
        ones_val=[None],
    )
    if atmp:
        ws["atmp"] = c.sb("ws_atmp", [128, T], BF16)
    return ws


def _ws_init(c, ws):
    P = c.P
    if ws["ones_val"][0] is None:
        P.op("vector", lambda E: E.memset(ws["ones"][:], 1.0))
        ws["ones_val"][0] = P.op("vector", lambda E: E.memset(ws["eps"][:], EPS))
        P.wv("tensor", ws["ones_val"][0])


def _ln_finalize(c, ws, name, pbanks, mean, st_ev):
    P = c.P
    m32, v32, w32 = ws["m32"], ws["v32"], ws["w32"]
    inv = 1.0 / D
    P.wait("vector", st_ev[0], st_ev[1])
    psfree_val = None
    for t in range(4):
        sl = slice(t * 512, (t + 1) * 512)
        if mean:
            P.op("vector", lambda E, b=pbanks[t], sl=sl: E.tensor_scalar(
                m32[:, sl], b[:], inv, None, ALU.mult, ALU.bypass))
        psfree_val = P.op("vector", lambda E, b=pbanks[4 + t], sl=sl: E.tensor_scalar(
            v32[:, sl], b[:], inv, None, ALU.mult, ALU.bypass))
    if mean:
        P.op("vector", lambda E: E.tensor_tensor(w32[:], m32[:], m32[:], ALU.mult))
        vready = P.op("vector", lambda E: E.tensor_tensor(v32[:], v32[:], w32[:], ALU.subtract))
    else:
        vready = psfree_val
    P.wv("scalar", vready)
    sqv = P.op("scalar", lambda E: E.activation(v32[:], v32[:], AF.Sqrt, bias=ws["eps"][:]))
    P.wsc("vector", sqv)
    P.op("vector", lambda E: E.reciprocal(v32[:], v32[:]))
    if mean:
        P.op("vector", lambda E: E.tensor_copy(ws["rb"][:], v32[:]))
        fin_val = P.op("vector", lambda E: E.tensor_tensor(ws["mrb"][:], m32[:], v32[:], ALU.mult))
    else:
        fin_val = P.op("vector", lambda E: E.tensor_copy(ws["rb"][:], v32[:]))
    return fin_val, psfree_val


def plan_ln_stats(c, ws, name, xb, ready, ready_vals, pbanks, mean=True):
    P = c.P
    _ws_init(c, ws)
    sq = ws["sq"]
    s_sqc = c.sem(name + "_ssqc")
    s_st = c.sem(name + "_sst")
    sqvals = []

    def wait_ready(eng, k):
        if ready == "vector":
            P.wv(eng, ready_vals[k])
        else:
            P.wait(eng, ready, ready_vals[k])

    for k in range(NCH):
        wait_ready("vector", k)
        if k >= 1:
            P.wait("vector", s_sqc, k)
        sl = slice(k * T, (k + 1) * T)
        sqvals.append(P.op("vector", lambda E, sl=sl: E.tensor_tensor(
            sq[0][:], xb[:, sl], xb[:, sl], ALU.mult)))

        wait_ready("tensor", k)
        P.wv("tensor", sqvals[k])
        fs, ls = (k == 0), (k == NCH - 1)
        for t in range(4):
            if mean:
                P.op("tensor", lambda E, b=pbanks[t], k=k, t=t, fs=fs, ls=ls: E.matmul(
                    b[:], ws["ones"][:], xb[:, k * T + t * 512: k * T + (t + 1) * 512],
                    start=fs, stop=ls))
            P.op("tensor", lambda E, b=pbanks[4 + t], t=t, fs=fs, ls=ls: E.matmul(
                b[:], ws["ones"][:], sq[0][:, t * 512:(t + 1) * 512],
                start=fs, stop=ls),
                (s_st if ls else s_sqc) if t == 3 else None)
    return _ln_finalize(c, ws, name, pbanks, mean, (s_st, 1))


def plan_stream_ln_stats(c, ws, name, src_dram, pbanks, mean=True, cast=True,
                         src_ready=None):
    P = c.P
    _ws_init(c, ws)
    ring = [c.sb(name + f"_xr{i}", [128, T], BF16) for i in range(2)]
    sq = ws["sq"]
    ld = c.dsem(name + "_ld")
    s_xc = c.sem(name + "_sxc")
    s_st = c.sem(name + "_sst")
    eng = "gpsimd" if cast else "sync"
    if src_ready is not None:
        src_ready(eng)
    for k in range(NCH):
        if k >= 2:
            P.wait(eng, s_xc, k - 1)
        tok = ld.issue(P, eng, ring[k % 2][:], src_dram[k * 128:(k + 1) * 128, :])
        ld.wait(P, "vector", tok)
        if k >= 1:
            P.wait("vector", s_xc, k)
        sqv = P.op("vector", lambda E, r=ring[k % 2]: E.tensor_tensor(
            sq[0][:], r[:], r[:], ALU.mult))
        ld.wait(P, "tensor", tok)
        P.wv("tensor", sqv)
        fs, ls = (k == 0), (k == NCH - 1)
        for t in range(4):
            if mean:
                P.op("tensor", lambda E, b=pbanks[t], r=ring[k % 2], t=t, fs=fs, ls=ls: E.matmul(
                    b[:], ws["ones"][:], r[:, t * 512:(t + 1) * 512], start=fs, stop=ls))
            P.op("tensor", lambda E, b=pbanks[4 + t], t=t, fs=fs, ls=ls: E.matmul(
                b[:], ws["ones"][:], sq[0][:, t * 512:(t + 1) * 512], start=fs, stop=ls),
                (s_st if ls else s_xc) if t == 3 else None)
    return _ln_finalize(c, ws, name, pbanks, mean, (s_st, 1))


def plan_ln_apply_inplace(c, ws, name, xb, fin_val):
    P = c.P
    tmp = ws["atmp"]
    P.wv("vector", fin_val)
    vals = []
    for k in range(NCH):
        sl = slice(k * T, (k + 1) * T)
        P.op("vector", lambda E, sl=sl: E.tensor_tensor(tmp[:], xb[:, sl], ws["rb"][:], ALU.mult))
        vals.append(P.op("vector", lambda E, sl=sl: E.tensor_tensor(
            xb[:, sl], tmp[:], ws["mrb"][:], ALU.subtract)))
    return vals


def plan_stream_tail(c, ws, name, src_dram, x_dram, xout_dram, pbanks, src_ready):
    P = c.P
    fin_val, _ = plan_stream_ln_stats(c, ws, name + "s", src_dram, pbanks,
                                      cast=False, src_ready=src_ready)
    ring = [c.sb(name + f"_ar{i}", [128, T], BF16) for i in range(2)]
    x32 = [c.sb(name + f"_ax{i}", [128, T], F32) for i in range(2)]
    t1 = c.sb(name + "_at1", [128, T], BF16)
    stage = c.sb(name + "_as0", [128, T], F32)
    lda = c.dsem(name + "_lda")
    ldx = c.dsem(name + "_ldx")
    st = c.dsem(name + "_st")
    stoks = {}
    resvals = {}
    P.wv("vector", fin_val)
    for k in range(NCH):
        if k >= 2:
            P.wv("sync", resvals[k - 2])
        ta = lda.issue(P, "sync", ring[k % 2][:], src_dram[k * 128:(k + 1) * 128, :])
        tx = ldx.issue(P, "sync", x32[k % 2][:], x_dram[k * 128:(k + 1) * 128, :])
        lda.wait(P, "vector", ta)
        ldx.wait(P, "vector", tx)
        if k >= 1:
            st.wait(P, "vector", stoks[k - 1])
        P.op("vector", lambda E, k=k: E.tensor_tensor(t1[:], ring[k % 2][:], ws["rb"][:], ALU.mult))
        P.op("vector", lambda E: E.tensor_tensor(t1[:], t1[:], ws["mrb"][:], ALU.subtract))
        resvals[k] = P.op("vector", lambda E, k=k: E.tensor_tensor(
            stage[:], x32[k % 2][:], t1[:], ALU.add))
        P.wv("sync", resvals[k])
        stoks[k] = st.issue(P, "sync", xout_dram[k * 128:(k + 1) * 128, :], stage[:])
    st.wait_all(P, "sync")



def plan_proj(c, name, w_dram, rhs_buf, nk, nm, width, rhs_ready, epilogue,
              pbanks, first_wait=(), cast=True):
    P = c.P
    nsub = width // 512
    wv = pview(w_dram)
    slab = [c.sb(name + f"_w{i}", [128, nk * 256], BF16) for i in range(2)]
    wl = c.dsem(name + "_wl")
    s_mm = c.sem(name + "_smm")
    ld_eng = "gpsimd" if cast else "sync"
    mmpb = nk * nsub
    wtoks = []
    evs = []

    def wait_ev(eng, ev):
        if ev[0] == "vector":
            P.wv(eng, ev[1])
        elif ev[0] == "scalar":
            P.wsc(eng, ev[1])
        else:
            P.wait(eng, ev[0], ev[1])

    for j in range(nm // 2):
        if j >= 2:
            P.wait(ld_eng, s_mm, (2 * j - 2) * mmpb)
        wtoks.append(wl.issue(P, ld_eng, slab[j % 2][:].rearrange("p (c m) -> p c m", c=nk),
                              wv[:, :, 2 * j * 128:(2 * j + 2) * 128]))

    for ev in first_wait:
        wait_ev("tensor", ev)
    for M in range(nm):
        j, m = M // 2, M % 2
        wl.wait(P, "tensor", wtoks[j])
        if M >= 2:
            wait_ev("tensor", evs[M - 2])
        bs = pbanks[0:4] if M % 2 == 0 else pbanks[4:8]
        for K in range(nk):
            if M == 0 and rhs_ready is not None:
                if rhs_ready[0] == "vector":
                    P.wv("tensor", rhs_ready[1][K])
                else:
                    P.wait("tensor", rhs_ready[0], rhs_ready[1][K])
            for t in range(nsub):
                P.op("tensor", lambda E, b=bs[t], j=j, K=K, m=m, t=t: E.matmul(
                    b[:], slab[j % 2][:, K * 256 + m * 128: K * 256 + (m + 1) * 128],
                    rhs_buf[:, K * width + t * 512: K * width + (t + 1) * 512],
                    start=(K == 0), stop=(K == nk - 1)), s_mm)
        evs.append(epilogue(M, bs, s_mm.v, s_mm))
    return s_mm, evs[-1]


def make_store_epilogue(c, name, out_dram, func, bias_cols=None, scale_cols=None,
                        out_dtype=F32):
    P = c.P
    stage = [c.sb(name + f"_est{i}", [128, T], out_dtype) for i in range(2)]
    st = c.dsem(name + "_esr")
    idx = [0]
    toks = {}

    def epilogue(M, banks, mm_val, s_mm):
        i = idx[0]
        idx[0] += 1
        P.wait("scalar", s_mm, mm_val)
        if i >= 2:
            st.wait(P, "scalar", toks[i - 2])
        stg_val = None
        for t in range(4):
            kw = {}
            if bias_cols is not None:
                kw["bias"] = bias_cols[:, M:M + 1]
            if scale_cols is not None:
                kw["scale"] = scale_cols[:, M:M + 1]
            stg_val = P.op("scalar", lambda E, b=banks[t], i=i, t=t, kw=kw: E.activation(
                stage[i % 2][:, t * 512:(t + 1) * 512], b[:], func, **kw))
        P.wsc("sync", stg_val)
        toks[i] = st.issue(P, "sync", out_dram[M * 128:(M + 1) * 128, :], stage[i % 2][:])
        return ("scalar", stg_val)

    def fin(eng="sync"):
        st.wait_all(P, eng)

    return epilogue, fin



def build_l0a_proj():
    c = Ctx()
    P = c.P
    xin = c.din("xin", [D, T])
    wts = [c.din(n, [D, D]) for n in ("wv", "wq", "wg")]
    v_o = c.dout("v_o", [D, T])
    q_o = c.dout("q_o", [D, T], BF16)
    g_o = c.dout("g_o", [D, T], BF16)

    ws = make_ws(c, atmp=True)
    xb = c.sb("xb", [128, NCH * T], BF16)
    pbanks = [c.ps(f"pb{i}", [128, 512]) for i in range(8)]
    s_xld = c.sem("sxld")
    for k in range(NCH):
        P.dma("gpsimd", xb[:, k * T:(k + 1) * T], xin[k * 128:(k + 1) * 128, :], s_xld)
    xvals = [s_xld.v] * NCH

    fin_val, psfree_val = plan_ln_stats(c, ws, "ln", xb, s_xld, xvals, pbanks)
    lnxvals = plan_ln_apply_inplace(c, ws, "ln", xb, fin_val)

    prev_ev = None
    fins = []
    for wi, (w, o, f, dt) in enumerate(zip(wts, (v_o, q_o, g_o),
                                           (AF.Copy, AF.Copy, AF.Silu),
                                           (F32, BF16, BF16))):
        ep, fin = make_store_epilogue(c, f"p{wi}", o, f, out_dtype=dt)
        fw = [("vector", psfree_val)] if wi == 0 else [prev_ev]
        _, prev_ev = plan_proj(c, f"p{wi}", w, xb, NCH, NCH, T,
                               ("vector", lnxvals) if wi == 0 else None,
                               ep, pbanks, first_wait=fw)
        fins.append(fin)
    for fin in fins:
        fin()
    return c.build()


def build_scan(with_lam):
    c = Ctx()
    P = c.P
    if with_lam:
        v_in = c.din("v_in", [8 * 128, S])
        betas = c.din("betasel", [128, 8])
    else:
        v_in = c.din("v_in", [8 * 128, S], BF16)
        g_in = c.din("g_in", [8 * 128, S])
    st_out = c.dout("st_out", [8 * 128, S])

    vt = [c.sb(f"vt{i}", [128, S], F32) for i in range(2)]
    gt = [c.sb(f"gt{i}", [128, S], F32) for i in range(2)] if not with_lam else None
    ot = [c.sb(f"ot{i}", [128, S], F32) for i in range(2)]
    ldv = c.dsem("ldv")
    ldg = c.dsem("ldg")
    st = c.dsem("st")
    stoks = {}
    scvals = {}

    if with_lam:
        lamb = c.sb("lamb", [128, 8], F32)
        s_bld = c.sem("sbld")
        P.dma("sync", lamb[:], betas, s_bld)
        P.wait("scalar", s_bld, 16)
        lam_val = P.op("scalar", lambda E: E.activation(lamb[:], lamb[:], AF.Sigmoid))

    for u in range(8):
        if u >= 2:
            P.wv("sync", scvals[u - 2])
            if not with_lam:
                P.wv("gpsimd", scvals[u - 2])
        if with_lam:
            tv = ldv.issue(P, "sync", vt[u % 2][:], v_in[u * 128:(u + 1) * 128, :])
            ldv.wait(P, "vector", tv)
            P.wsc("vector", lam_val)
            d0 = lamb[:, u:u + 1].broadcast_to([128, S])
        else:
            tv = ldv.issue(P, "gpsimd", vt[u % 2][:], v_in[u * 128:(u + 1) * 128, :])
            tg = ldg.issue(P, "sync", gt[u % 2][:], g_in[u * 128:(u + 1) * 128, :])
            ldv.wait(P, "vector", tv)
            ldg.wait(P, "vector", tg)
            d0 = gt[u % 2][:]
        if u >= 2:
            st.wait(P, "vector", stoks[u - 2])
        scvals[u] = P.op("vector", lambda E, d0=d0, u=u: E.tensor_tensor_scan(
            ot[u % 2][:], d0, vt[u % 2][:], 0.0, ALU.mult, ALU.add))
        P.wv("sync", scvals[u])
        stoks[u] = st.issue(P, "sync", st_out[u * 128:(u + 1) * 128, :], ot[u % 2][:])
    st.wait_all(P, "sync")
    return c.build()


def build_l0a_post():
    c = Ctx()
    P = c.P
    xin = c.din("xin", [D, T])
    q_in = c.din("q_in", [D, T], BF16)
    st_in = c.din("st_in", [D, T])
    gate_in = c.din("gate_in", [D, T], BF16)
    lng = c.din("lng_cols", [128, NCH])
    lnb = c.din("lnb_cols", [128, NCH])
    wo = c.din("wo", [D, D])
    x1_o = c.dout("x1_o", [D, T])
    aos = c.dtmp("aos", [D, T], BF16)

    ws = make_ws(c)
    pbanks = [c.ps(f"pb{i}", [128, 512]) for i in range(8)]
    ybuf = c.sb("ybuf", [128, NCH * T], BF16)
    qb = [c.sb(f"qb{i}", [128, T], BF16) for i in range(2)]
    stb = [c.sb(f"stb{i}", [128, T], BF16) for i in range(2)]
    gb = [c.sb(f"gb{i}", [128, T], BF16) for i in range(2)]
    t2b = c.sb("t2b", [128, T], BF16)
    lngc = c.sb("lngc", [128, NCH], F32)
    lnbc = c.sb("lnbc", [128, NCH], F32)

    s_cold = c.sem("scold")
    ldq = c.dsem("ldq")
    lds = c.dsem("lds")
    ldg = c.dsem("ldg")

    P.dma("sync", lngc[:], lng, s_cold)
    P.dma("sync", lnbc[:], lnb, s_cold)

    yvals = []
    for k in range(NCH):
        if k >= 2:
            P.wv("sync", yvals[k - 2])
            P.wv("gpsimd", yvals[k - 2])
        tq = ldq.issue(P, "sync", qb[k % 2][:], q_in[k * 128:(k + 1) * 128, :])
        ts_ = lds.issue(P, "gpsimd", stb[k % 2][:], st_in[k * 128:(k + 1) * 128, :])
        ldq.wait(P, "vector", tq)
        lds.wait(P, "vector", ts_)
        yvals.append(P.op("vector", lambda E, k=k: E.tensor_tensor(
            ybuf[:, k * T:(k + 1) * T], qb[k % 2][:], stb[k % 2][:], ALU.mult)))

    fin_val, psfree_val = plan_ln_stats(c, ws, "lny", ybuf, "vector", yvals, pbanks)

    y2vals = []
    P.wait("vector", s_cold, 32)
    P.wv("vector", fin_val)
    for k in range(NCH):
        if k >= 2:
            P.wv("sync", y2vals[k - 2])
        tg = ldg.issue(P, "sync", gb[k % 2][:], gate_in[k * 128:(k + 1) * 128, :])
        sl = slice(k * T, (k + 1) * T)
        P.op("vector", lambda E, sl=sl: E.tensor_tensor(t2b[:], ybuf[:, sl], ws["rb"][:], ALU.mult))
        P.op("vector", lambda E: E.tensor_tensor(t2b[:], t2b[:], ws["mrb"][:], ALU.subtract))
        P.op("vector", lambda E, k=k: E.tensor_scalar(
            t2b[:], t2b[:], lngc[:, k:k + 1], lnbc[:, k:k + 1], ALU.mult, ALU.add))
        ldg.wait(P, "vector", tg)
        y2vals.append(P.op("vector", lambda E, sl=sl, k=k: E.tensor_tensor(
            ybuf[:, sl], t2b[:], gb[k % 2][:], ALU.mult)))

    ep, fin = make_store_epilogue(c, "ao", aos, AF.Copy, out_dtype=BF16)
    plan_proj(c, "po", wo, ybuf, NCH, NCH, T, ("vector", y2vals), ep, pbanks,
              first_wait=[("vector", psfree_val)])
    plan_stream_tail(c, ws, "tl", aos, xin, x1_o, pbanks, lambda eng: fin(eng))
    return c.build()


def build_l1a_proj():
    c = Ctx()
    P = c.P
    xin = c.din("xin", [D, T])
    wi_ = c.din("wi", [D, D])
    wf_ = c.din("wf", [D, D])
    wg_ = c.din("wg", [D, D])
    vv_o = c.dout("vv_o", [D, T], BF16)
    gg_o = c.dout("gg_o", [D, T])
    g2_o = c.dout("g2_o", [D, T], BF16)

    ws = make_ws(c, atmp=True)
    xb = c.sb("xb", [128, NCH * T], BF16)
    pbanks = [c.ps(f"pb{i}", [128, 512]) for i in range(8)]
    s_xld = c.sem("sxld")
    for k in range(NCH):
        P.dma("gpsimd", xb[:, k * T:(k + 1) * T], xin[k * 128:(k + 1) * 128, :], s_xld)
    xvals = [s_xld.v] * NCH
    fin_val, psfree_val = plan_ln_stats(c, ws, "ln", xb, s_xld, xvals, pbanks)
    lnxvals = plan_ln_apply_inplace(c, ws, "ln", xb, fin_val)

    wiv, wfv = pview(wi_), pview(wf_)
    islab = [c.sb(f"isl{i}", [128, NCH * 256], BF16) for i in range(2)]
    fslab = [c.sb(f"fsl{i}", [128, NCH * 256], BF16) for i in range(2)]
    silb = c.sb("silb", [128, T], BF16)
    omb = c.sb("omb", [128, T], BF16)
    ggst = [c.sb(f"ggst{i}", [128, T], F32) for i in range(2)]
    vvst = [c.sb(f"vvst{i}", [128, T], BF16) for i in range(2)]
    wil = c.dsem("wil")
    wfl = c.dsem("wfl")
    s_mif = c.sem("smif")
    stg = c.dsem("stg")
    stv = c.dsem("stv")
    iw, fw_ = [], []
    gtoks, vtoks = {}, {}
    vvvals = {}
    sigvals = {}

    for j in range(NCH // 2):
        if j >= 2:
            P.wait("gpsimd", s_mif, (2 * j - 2) * 2 * NCH * 4)
        iw.append(wil.issue(P, "gpsimd", islab[j % 2][:].rearrange("p (c m) -> p c m", c=NCH),
                            wiv[:, :, 2 * j * 128:(2 * j + 2) * 128]))
        fw_.append(wfl.issue(P, "gpsimd", fslab[j % 2][:].rearrange("p (c m) -> p c m", c=NCH),
                             wfv[:, :, 2 * j * 128:(2 * j + 2) * 128]))

    P.wv("tensor", lnxvals[-1])
    P.wv("tensor", psfree_val)
    for M in range(NCH):
        j, m = M // 2, M % 2
        wil.wait(P, "tensor", iw[j])
        wfl.wait(P, "tensor", fw_[j])
        if M >= 1:
            P.wsc("tensor", sigvals[M - 1])
        for K in range(NCH):
            for t in range(4):
                P.op("tensor", lambda E, b=pbanks[t], j=j, K=K, m=m, t=t: E.matmul(
                    b[:], islab[j % 2][:, K * 256 + m * 128: K * 256 + (m + 1) * 128],
                    xb[:, K * T + t * 512: K * T + (t + 1) * 512],
                    start=(K == 0), stop=(K == NCH - 1)), s_mif)
        for K in range(NCH):
            for t in range(4):
                P.op("tensor", lambda E, b=pbanks[4 + t], j=j, K=K, m=m, t=t: E.matmul(
                    b[:], fslab[j % 2][:, K * 256 + m * 128: K * 256 + (m + 1) * 128],
                    xb[:, K * T + t * 512: K * T + (t + 1) * 512],
                    start=(K == 0), stop=(K == NCH - 1)), s_mif)
        P.wait("scalar", s_mif, s_mif.v)
        if M >= 2:
            stg.wait(P, "scalar", gtoks[M - 2])
            stv.wait(P, "scalar", vtoks[M - 2])
        if M >= 1:
            P.wv("scalar", vvvals[M - 1])
        sil_val = None
        for t in range(4):
            sil_val = P.op("scalar", lambda E, b=pbanks[t], t=t: E.activation(
                silb[:, t * 512:(t + 1) * 512], b[:], AF.Silu))
        sig_val = None
        for t in range(4):
            sig_val = P.op("scalar", lambda E, b=pbanks[4 + t], M=M, t=t: E.activation(
                ggst[M % 2][:, t * 512:(t + 1) * 512], b[:], AF.Sigmoid))
        sigvals[M] = sig_val
        P.wsc("vector", sig_val)
        if M >= 2:
            stg.wait(P, "vector", gtoks[M - 2])
            stv.wait(P, "vector", vtoks[M - 2])
        P.op("vector", lambda E, M=M: E.tensor_scalar(
            omb[:], ggst[M % 2][:], -1.0, 1.0, ALU.mult, ALU.add))
        vvvals[M] = P.op("vector", lambda E, M=M: E.tensor_tensor(
            vvst[M % 2][:], silb[:], omb[:], ALU.mult))
        P.wsc("sync", sig_val)
        gtoks[M] = stg.issue(P, "sync", gg_o[M * 128:(M + 1) * 128, :], ggst[M % 2][:])
        P.wv("sync", vvvals[M])
        vtoks[M] = stv.issue(P, "sync", vv_o[M * 128:(M + 1) * 128, :], vvst[M % 2][:])

    ep2, fin2 = make_store_epilogue(c, "pge", g2_o, AF.Silu, out_dtype=BF16)
    plan_proj(c, "pg", wg_, xb, NCH, NCH, T, None, ep2, pbanks,
              first_wait=[("scalar", sigvals[NCH - 1]), ("vector", vvvals[NCH - 1])])
    fin2()
    stg.wait_all(P, "sync")
    stv.wait_all(P, "sync")
    return c.build()


def build_l1a_post():
    c = Ctx()
    P = c.P
    xin = c.din("xin", [D, T])
    o_in = c.din("o_in", [D, T])
    g2_in = c.din("g2_in", [D, T], BF16)
    gnw = c.din("gnw_cols", [128, NCH])
    wo = c.din("wo", [D, D])
    x3_o = c.dout("x3_o", [D, T])
    aos = c.dtmp("aos", [D, T], BF16)

    ws = make_ws(c)
    pbanks = [c.ps(f"pb{i}", [128, 512]) for i in range(8)]
    obuf = c.sb("obuf", [128, NCH * T], BF16)
    g2b = [c.sb(f"g2b{i}", [128, T], BF16) for i in range(2)]
    t2b = c.sb("t2b", [128, T], BF16)
    gnwc = c.sb("gnwc", [128, NCH], F32)

    s_cold = c.sem("scold")
    s_old = c.sem("sold")
    ldg = c.dsem("ldg")
    P.dma("sync", gnwc[:], gnw, s_cold)

    for k in range(NCH):
        P.dma("gpsimd", obuf[:, k * T:(k + 1) * T], o_in[k * 128:(k + 1) * 128, :], s_old)
    ovals = [s_old.v] * NCH
    fin_val, psfree_val = plan_ln_stats(c, ws, "rms", obuf, s_old, ovals,
                                        pbanks, mean=False)
    y2vals = []
    P.wait("vector", s_cold, 16)
    P.wv("vector", fin_val)
    for k in range(NCH):
        if k >= 2:
            P.wv("sync", y2vals[k - 2])
        tg = ldg.issue(P, "sync", g2b[k % 2][:], g2_in[k * 128:(k + 1) * 128, :])
        sl = slice(k * T, (k + 1) * T)
        P.op("vector", lambda E, sl=sl: E.tensor_tensor(t2b[:], obuf[:, sl], ws["rb"][:], ALU.mult))
        P.op("vector", lambda E, k=k: E.tensor_scalar(
            t2b[:], t2b[:], gnwc[:, k:k + 1], None, ALU.mult, ALU.bypass))
        ldg.wait(P, "vector", tg)
        y2vals.append(P.op("vector", lambda E, sl=sl, k=k: E.tensor_tensor(
            obuf[:, sl], t2b[:], g2b[k % 2][:], ALU.mult)))

    ep, fin = make_store_epilogue(c, "ao", aos, AF.Copy, out_dtype=BF16)
    plan_proj(c, "po", wo, obuf, NCH, NCH, T, ("vector", y2vals), ep, pbanks,
              first_wait=[("vector", psfree_val)])
    plan_stream_tail(c, ws, "tl", aos, xin, x3_o, pbanks, lambda eng: fin(eng))
    return c.build()



def _half_slabs(w_dram, slab_w):
    wv = wview(w_dram)
    nrow = w_dram.shape[0] // 128
    per = w_dram.shape[1] // slab_w
    out = []
    for i in range(nrow):
        for j in range(per):
            out.append(wv[i][:, j * slab_w:(j + 1) * slab_w])
    return out


def plan_weight_absmean(c, name, w_dram, slab_w, pbank, tcol, ntcol, scale):
    P = c.P
    srcs = _half_slabs(w_dram, slab_w)
    nslab = len(srcs)
    slab = [c.sb(name + f"_qs{i}", [128, slab_w], F32) for i in range(2)]
    parts = c.sb(name + "_parts", [128, nslab], F32)
    part1 = c.sb(name + "_part1", [128, 1], F32)
    ones32 = c.sb(name + "_o32", [128, 1], F32)
    ones32r = c.sb(name + "_o32r", [1, 128], F32)
    tmp11 = c.sb(name + "_t11", [1, 1], F32)
    ld = c.dsem(name + "_ld")
    s_mm = c.sem(name + "_sqmm")

    P.op("vector", lambda E: E.memset(ones32[:], 1.0))
    P.op("vector", lambda E: E.memset(ones32r[:], 1.0))
    redvals = {}
    for i in range(nslab):
        if i >= 2:
            P.wv("sync", redvals[i - 2])
        tok = ld.issue(P, "sync", slab[i % 2][:], srcs[i])
        ld.wait(P, "vector", tok)
        redvals[i] = P.op("vector", lambda E, i=i: E.tensor_reduce(
            parts[:, i:i + 1], slab[i % 2][:], axis=mybir.AxisListType.X, op=ALU.add,
            apply_absolute_value=True))
    p1v = P.op("vector", lambda E: E.tensor_reduce(
        part1[:], parts[:], axis=mybir.AxisListType.X, op=ALU.add))
    P.wv("tensor", p1v)
    P.op("tensor", lambda E: E.matmul(pbank[0:1, 0:1], ones32[:], part1[:],
                                      start=True, stop=True), s_mm)
    P.wait("vector", s_mm, 1)
    t11v = P.op("vector", lambda E: E.tensor_scalar(
        tmp11[:], pbank[0:1, 0:1], scale, None, ALU.mult, ALU.bypass))
    P.wv("tensor", t11v)
    P.op("tensor", lambda E: E.matmul(pbank[:, 0:1], ones32r[:], tmp11[:],
                                      start=True, stop=True), s_mm)
    P.wait("vector", s_mm, 2)
    P.op("vector", lambda E: E.tensor_copy(tcol[:], pbank[:, 0:1]))
    tv = P.op("vector", lambda E: E.tensor_scalar(ntcol[:], pbank[:, 0:1], -1.0, None,
                                                  ALU.mult, ALU.bypass))
    return tv


def plan_quantize(c, name, w_dram, wq_dram, slab_w, tcol, ntcol, t_val):
    P = c.P
    srcs = _half_slabs(w_dram, slab_w)
    dsts = _half_slabs(wq_dram, slab_w)
    nslab = len(srcs)
    slab = [c.sb(name + f"_zs{i}", [128, slab_w], F32) for i in range(2)]
    mp = c.sb(name + "_mp", [128, slab_w], BF16)
    mn = c.sb(name + "_mn", [128, slab_w], BF16)
    wq = [c.sb(name + f"_wq{i}", [128, slab_w], BF16) for i in range(2)]
    ld = c.dsem(name + "_ld")
    st = c.dsem(name + "_st")
    stoks = {}
    qvals = {}
    P.wv("vector", t_val)
    for i in range(nslab):
        if i >= 2:
            P.wv("sync", qvals[i - 1])
        tok = ld.issue(P, "sync", slab[i % 2][:], srcs[i])
        ld.wait(P, "vector", tok)
        if i >= 2:
            st.wait(P, "vector", stoks[i - 2])
        P.op("vector", lambda E, i=i: E.tensor_scalar(
            mp[:], slab[i % 2][:], tcol[:], None, ALU.is_ge, ALU.bypass))
        P.op("vector", lambda E, i=i: E.tensor_scalar(
            mn[:], slab[i % 2][:], ntcol[:], None, ALU.is_le, ALU.bypass))
        qvals[i] = P.op("vector", lambda E, i=i: E.tensor_tensor(
            wq[i % 2][:], mp[:], mn[:], ALU.subtract))
        P.wv("gpsimd", qvals[i])
        stoks[i] = st.issue(P, "gpsimd", dsts[i], wq[i % 2][:])
    return st


def build_l0f_quant():
    c = Ctx()
    P = c.P
    w1t = c.din("w1t", [D, DFF])
    w2t = c.din("w2t", [DFF, D])
    w1sl = c.din("w1sl", [D // 8, DFF])
    w2sl = c.din("w2sl", [DFF // 8, D])
    w1q_o = c.dout("w1q_o", [D // 8, DFF], BF16)
    w2q_o = c.dout("w2q_o", [DFF // 8, D], BF16)

    pb = c.ps("pb0", [128, 512])
    tcol1 = c.sb("tcol1", [128, 1], F32)
    ntcol1 = c.sb("ntcol1", [128, 1], F32)
    tcol2 = c.sb("tcol2", [128, 1], F32)
    ntcol2 = c.sb("ntcol2", [128, 1], F32)

    t1v = plan_weight_absmean(c, "q1", w1t, 4096, pb, tcol1, ntcol1,
                              0.7 / (D * DFF))
    st1 = plan_quantize(c, "z1", w1sl, w1q_o, 4096, tcol1, ntcol1, t1v)
    t2v = plan_weight_absmean(c, "q2", w2t, 2048, pb, tcol2, ntcol2,
                              0.7 / (D * DFF))
    st2 = plan_quantize(c, "z2", w2sl, w2q_o, 2048, tcol2, ntcol2, t2v)
    st1.wait_all(P, "sync")
    st2.wait_all(P, "sync")
    return c.build()


def build_ffn(quant):
    c = Ctx()
    P = c.P
    xin = c.din("xin", [D, T])
    if quant:
        wa = c.din("w1q", [D, DFF], BF16)
        wb_ = c.din("w2q", [DFF, D], BF16)
        b1c = c.din("b1_cols", [128, FCH])
        b2c = c.din("b2_cols", [128, NCH])
    else:
        wa = c.din("wgt", [D, DFF])
        wu = c.din("wut", [D, DFF])
        wb_ = c.din("wdt", [DFF, D])
    xout = c.dout("x_o", [D, T])
    ffo = c.dtmp("ffo", [D, T], BF16)

    TAU = 4
    TW = T // TAU
    ws = make_ws(c)
    pbanks = [c.ps(f"pb{i}", [128, 512]) for i in range(8)]

    if quant:
        b1cc = c.sb("b1cc", [128, FCH], F32)
        b2cc = c.sb("b2cc", [128, NCH], F32)
        s_cold = c.sem("scold")
        P.dma("sync", b1cc[:], b1c, s_cold)
        P.dma("sync", b2cc[:], b2c, s_cold)
        P.wait("scalar", s_cold, 32)

    fin_val, psfree_val = plan_stream_ln_stats(c, ws, "ln", xin, pbanks)
    lnxt = c.sb("lnxt", [128, NCH * TW], BF16)
    ltmp = c.sb("ltmp", [128, TW], BF16)
    s_lld = c.sem("slld")
    lnx_gate = {}
    xv3 = pview(xin)

    def plan_lnx_tau(tau):
        if tau >= 1:
            g = lnx_gate[tau]
            P.wait("gpsimd", g[0], g[1])
        P.dma("gpsimd", lnxt[:].rearrange("p (c w) -> p c w", c=NCH),
              xv3[:, :, tau * TW:(tau + 1) * TW], s_lld)
        P.wait("vector", s_lld, 16 * (tau + 1))
        P.wv("vector", fin_val)
        if tau >= 1:
            g = lnx_gate[tau]
            P.wait("vector", g[0], g[1])
        last = None
        for k in range(NCH):
            sl = slice(k * TW, (k + 1) * TW)
            tsl = slice(tau * TW, (tau + 1) * TW)
            P.op("vector", lambda E, sl=sl, tsl=tsl: E.tensor_tensor(
                ltmp[:], lnxt[:, sl], ws["rb"][:, tsl], ALU.mult))
            last = P.op("vector", lambda E, sl=sl, tsl=tsl: E.tensor_tensor(
                lnxt[:, sl], ltmp[:], ws["mrb"][:, tsl], ALU.subtract))
        return last

    hbuf = c.sb("hbuf", [128, FCH * TW], BF16)
    if quant:
        w1s = [c.sb(f"w1s{i}", [128, NCH * 256], BF16) for i in range(2)]
        w1l = c.dsem("w1l")
    else:
        wgs = [c.sb(f"wgs{i}", [128, NCH * 128], BF16) for i in range(2)]
        wus = [c.sb(f"wus{i}", [128, NCH * 128], BF16) for i in range(2)]
        sgb = c.sb("sgb", [128, TW], BF16)
        wgl = c.dsem("wgl")
        wul = c.dsem("wul")
        s_mu = c.sem("smu")
    w2s = [c.sb(f"w2s{i}", [128, 32 * 128], BF16) for i in range(2)]
    fst = [c.sb(f"fst{i}", [128, TW], BF16) for i in range(2)]
    w2l = c.dsem("w2l")
    s_m1 = c.sem("sm1")
    s_m2 = c.sem("sm2")
    stf = c.dsem("stf")
    ftoks = {}
    hvals = {}
    sgvals = {}
    fevvals = {}
    wav = pview(wa)
    wbv = pview(wb_)
    if not quant:
        wuv = pview(wu)

    def wait_h(eng, val):
        if quant:
            P.wsc(eng, val)
        else:
            P.wv(eng, val)

    P.wv("tensor", psfree_val)
    nw1 = 0
    nw2 = 0
    for tau in range(TAU):
        lnx_last = plan_lnx_tau(tau)
        P.wv("tensor", lnx_last)
        if quant:
            for jp in range(FCH // 2):
                nw1 += 1
                if nw1 > 2:
                    P.wait("sync", s_m1, (nw1 - 2) * 2 * NCH)
                t1tok = w1l.issue(P, "sync",
                                  w1s[nw1 % 2][:].rearrange("p (c m) -> p c m", c=NCH),
                                  wav[:, :, 2 * jp * 128:(2 * jp + 2) * 128])
                for m in range(2):
                    fM = 2 * jp + m
                    w1l.wait(P, "tensor", t1tok)
                    hidx = tau * FCH + fM
                    if hidx >= 2:
                        wait_h("tensor", hvals[hidx - 2])
                    bk = pbanks[fM % 2]
                    for K in range(NCH):
                        P.op("tensor", lambda E, bk=bk, n=nw1, K=K, m=m: E.matmul(
                            bk[:], w1s[n % 2][:, K * 256 + m * 128: K * 256 + (m + 1) * 128],
                            lnxt[:, K * TW:(K + 1) * TW],
                            start=(K == 0), stop=(K == NCH - 1)), s_m1)
                    P.wait("scalar", s_m1, s_m1.v)
                    hvals[hidx] = P.op("scalar", lambda E, bk=bk, fM=fM: E.activation(
                        hbuf[:, fM * TW:(fM + 1) * TW], bk[:], AF.Gelu,
                        bias=b1cc[:, fM:fM + 1]))
        else:
            for fM in range(FCH):
                nw1 += 1
                hidx = tau * FCH + fM
                if nw1 > 2:
                    wait_h("gpsimd", hvals[hidx - 2])
                tgk = wgl.issue(P, "gpsimd",
                                wgs[nw1 % 2][:].rearrange("p (c m) -> p c m", c=NCH),
                                wav[:, :, fM * 128:(fM + 1) * 128])
                tuk = wul.issue(P, "gpsimd",
                                wus[nw1 % 2][:].rearrange("p (c m) -> p c m", c=NCH),
                                wuv[:, :, fM * 128:(fM + 1) * 128])
                wgl.wait(P, "tensor", tgk)
                wul.wait(P, "tensor", tuk)
                if hidx >= 1:
                    P.wsc("tensor", sgvals[hidx - 1])
                    wait_h("tensor", hvals[hidx - 1])
                for K in range(NCH):
                    P.op("tensor", lambda E, n=nw1, K=K: E.matmul(
                        pbanks[0][:], wgs[n % 2][:, K * 128:(K + 1) * 128],
                        lnxt[:, K * TW:(K + 1) * TW],
                        start=(K == 0), stop=(K == NCH - 1)), s_m1)
                for K in range(NCH):
                    P.op("tensor", lambda E, n=nw1, K=K: E.matmul(
                        pbanks[1][:], wus[n % 2][:, K * 128:(K + 1) * 128],
                        lnxt[:, K * TW:(K + 1) * TW],
                        start=(K == 0), stop=(K == NCH - 1)), s_mu)
                P.wait("scalar", s_m1, s_m1.v)
                sgvals[hidx] = P.op("scalar", lambda E: E.activation(sgb[:], pbanks[0][:], AF.Silu))
                P.wsc("vector", sgvals[hidx])
                P.wait("vector", s_mu, s_mu.v)
                hvals[hidx] = P.op("vector", lambda E, fM=fM: E.tensor_tensor(
                    hbuf[:, fM * TW:(fM + 1) * TW], sgb[:], pbanks[1][:], ALU.mult))
        lnx_gate[tau + 1] = ((s_m1, s_m1.v) if quant else (s_mu, s_mu.v))
        for M in range(NCH):
            for half in range(2):
                nw2 += 1
                w2eng = "sync" if quant else "gpsimd"
                if nw2 > 2:
                    P.wait(w2eng, s_m2, (nw2 - 2) * 32)
                t2tok = w2l.issue(P, w2eng,
                                  w2s[nw2 % 2][:].rearrange("p (c m) -> p c m", c=32),
                                  wbv[:, 32 * half:32 * (half + 1), M * 128:(M + 1) * 128])
                w2l.wait(P, "tensor", t2tok)
                fevidx = tau * NCH + M
                if fevidx >= 2:
                    ev = fevvals[fevidx - 2]
                    (P.wv if ev[0] == "vector" else P.wsc)("tensor", ev[1])
                wait_h("tensor", hvals[(tau + 1) * FCH - 1])
                bk2 = pbanks[2 + (M % 2)]
                for Kf in range(32):
                    gK = 32 * half + Kf
                    P.op("tensor", lambda E, bk2=bk2, n=nw2, Kf=Kf, gK=gK: E.matmul(
                        bk2[:], w2s[n % 2][:, Kf * 128:(Kf + 1) * 128],
                        hbuf[:, gK * TW:(gK + 1) * TW],
                        start=(gK == 0), stop=(gK == FCH - 1)), s_m2)
            i = tau * NCH + M
            if quant:
                P.wait("vector", s_m2, s_m2.v)
                if i >= 2:
                    stf.wait(P, "vector", ftoks[i - 2])
                fevvals[i] = ("vector", P.op("vector",
                    lambda E, bk2=pbanks[2 + (M % 2)], M=M, i=i: E.tensor_scalar(
                        fst[i % 2][:], bk2[:], b2cc[:, M:M + 1], None, ALU.add, ALU.bypass)))
                P.wv("gpsimd", fevvals[i][1])
            else:
                P.wait("scalar", s_m2, s_m2.v)
                if i >= 2:
                    stf.wait(P, "scalar", ftoks[i - 2])
                fevvals[i] = ("scalar", P.op("scalar",
                    lambda E, bk2=pbanks[2 + (M % 2)], i=i: E.activation(
                        fst[i % 2][:], bk2[:], AF.Copy)))
                P.wsc("gpsimd", fevvals[i][1])
            ftoks[i] = stf.issue(P, "gpsimd",
                                 ffo[M * 128:(M + 1) * 128, tau * TW:(tau + 1) * TW],
                                 fst[i % 2][:])

    ev = fevvals[TAU * NCH - 1]
    (P.wv if ev[0] == "vector" else P.wsc)("tensor", ev[1])
    plan_stream_tail(c, ws, "tl", ffo, xin, xout, pbanks,
                     lambda eng: stf.wait_all(P, eng))
    return c.build()



def _toks(a):
    return [np.ascontiguousarray(a[:, cid * T:(cid + 1) * T]) for cid in range(NCORE)]


def _units(a):
    out = []
    for cid in range(NCORE):
        tiles = []
        for u in range(8):
            g = 8 * cid + u
            b, h = g // 16, g % 16
            tiles.append(a[h * 128:(h + 1) * 128, b * S:(b + 1) * S])
        out.append(np.ascontiguousarray(np.concatenate(tiles, axis=0)))
    return out


def _units_back(parts):
    full = np.empty((D, B * S), dtype=np.float32)
    for cid in range(NCORE):
        for u in range(8):
            g = 8 * cid + u
            b, h = g // 16, g % 16
            full[h * 128:(h + 1) * 128, b * S:(b + 1) * S] = \
                parts[cid][u * 128:(u + 1) * 128, :]
    return full


def _cols(v):
    return np.ascontiguousarray(np.asarray(v, np.float32).reshape(-1, 128).T)


def _run(nc, in_maps):
    return run_bass_kernel_spmd(nc, in_maps, core_ids=list(range(NCORE))).results


def kernel(**inp):
    f32 = np.float32

    def arr(name, transpose=False):
        a = np.asarray(inp[name], f32)
        if transpose:
            a = a.T
        return np.ascontiguousarray(a)

    x = np.ascontiguousarray(np.asarray(inp["L0_attn_x"], f32).reshape(B * S, D).T)

    nc = build_l0a_proj()
    res = _run(nc, [dict(xin=xc, wv=arr("L0_attn_Wv"), wq=arr("L0_attn_Wq"),
                         wg=arr("L0_attn_Wg")) for xc in _toks(x)])
    v_f = np.concatenate([r["v_o"] for r in res], axis=1)
    q_f = np.concatenate([r["q_o"] for r in res], axis=1)
    g_f = np.concatenate([r["g_o"] for r in res], axis=1)

    nc = build_scan(with_lam=True)
    beta = np.asarray(inp["L0_attn_beta"], f32)
    maps = []
    for cid, vc in enumerate(_units(v_f)):
        hs = [(8 * cid + u) % 16 for u in range(8)]
        bsel = np.ascontiguousarray(np.broadcast_to(beta[hs][None, :], (128, 8)))
        maps.append(dict(v_in=vc, betasel=bsel))
    res = _run(nc, maps)
    st_f = _units_back([r["st_out"] for r in res])

    nc = build_l0a_post()
    res = _run(nc, [dict(xin=xc, q_in=qc, st_in=sc, gate_in=gc,
                         lng_cols=_cols(inp["L0_attn_ln_g"]),
                         lnb_cols=_cols(inp["L0_attn_ln_b"]),
                         wo=arr("L0_attn_Wo"))
                    for xc, qc, sc, gc in zip(_toks(x), _toks(q_f), _toks(st_f), _toks(g_f))])
    x1 = np.concatenate([r["x1_o"] for r in res], axis=1)

    nc = build_l0f_quant()
    w1t = arr("L0_ffn_w1", True)
    w2t = arr("L0_ffn_w2", True)
    res = _run(nc, [dict(w1t=w1t, w2t=w2t,
                         w1sl=np.ascontiguousarray(w1t[cid * (D // 8):(cid + 1) * (D // 8)]),
                         w2sl=np.ascontiguousarray(w2t[cid * (DFF // 8):(cid + 1) * (DFF // 8)]))
                    for cid in range(NCORE)])
    w1q = np.concatenate([r["w1q_o"] for r in res], axis=0)
    w2q = np.concatenate([r["w2q_o"] for r in res], axis=0)

    nc = build_ffn(quant=True)
    res = _run(nc, [dict(xin=xc, w1q=w1q, w2q=w2q,
                         b1_cols=_cols(inp["L0_ffn_b1"]), b2_cols=_cols(inp["L0_ffn_b2"]))
                    for xc in _toks(x1)])
    x2 = np.concatenate([r["x_o"] for r in res], axis=1)

    nc = build_l1a_proj()
    res = _run(nc, [dict(xin=xc, wi=arr("L1_attn_Wi", True), wf=arr("L1_attn_Wf", True),
                         wg=arr("L1_attn_Wg", True)) for xc in _toks(x2)])
    vv = np.concatenate([r["vv_o"] for r in res], axis=1)
    gg = np.concatenate([r["gg_o"] for r in res], axis=1)
    g2 = np.concatenate([r["g2_o"] for r in res], axis=1)

    nc = build_scan(with_lam=False)
    res = _run(nc, [dict(v_in=vc, g_in=gc) for vc, gc in zip(_units(vv), _units(gg))])
    o_f = _units_back([r["st_out"] for r in res])

    nc = build_l1a_post()
    res = _run(nc, [dict(xin=xc, o_in=oc, g2_in=gc,
                         gnw_cols=_cols(inp["L1_attn_gnorm_w"]),
                         wo=arr("L1_attn_Wo", True))
                    for xc, oc, gc in zip(_toks(x2), _toks(o_f), _toks(g2))])
    x3 = np.concatenate([r["x3_o"] for r in res], axis=1)

    nc = build_ffn(quant=False)
    res = _run(nc, [dict(xin=xc, wgt=arr("L1_ffn_w_gate", True),
                         wut=arr("L1_ffn_w_up", True), wdt=arr("L1_ffn_w_down", True))
                    for xc in _toks(x3)])
    x4 = np.concatenate([r["x_o"] for r in res], axis=1)

    return np.ascontiguousarray(x4.T).reshape(B, S, D)
